# version 02
# baseline (speedup 1.0000x reference)
import contextlib
import numpy as np
import concourse.bass as bass
import concourse.mybir as mybir
from concourse.bass_utils import run_bass_kernel_spmd

F32 = mybir.dt.float32
I32 = mybir.dt.int32
I16 = mybir.dt.int16
AF = mybir.ActivationFunctionType
ALU = mybir.AluOpType
AX = mybir.AxisListType

D = 1024
T = 2048
DI = 3072
NCH = D // 128
NI = DI // 128
NT = T // 512
NCORES = 8

INV32767 = float(np.float32(1.0 / 32767.0))
INV1024 = float(np.float32(1.0 / 1024.0))
INV65536 = float(np.float32(1.0 / 65536.0))
INV7 = float(np.float32(1.0 / 7.0))

SCR_COLS = 11776
DBG = False
ATT_UPTO = 7
FFN_UPTO = 7
UPTO = 7


class Ctr:
    def __init__(self, h):
        self.h = h
        self.v = 0

    def inc(self, n):
        self.v += n
        return self.v


class Plan:
    ENGINES = ("tensor", "vector", "scalar", "sync", "gpsimd")

    def __init__(self):
        self.ops = {k: [] for k in self.ENGINES}

    def emit(self, eng, fn):
        self.ops[eng].append(fn)

    def wait(self, eng, sem, val):
        if val and val > 0:
            self.emit(eng, lambda e, s=sem.h, v=val: e.wait_ge(s, v))


def build_program():
    nc = bass.Bass("TRN2", target_bir_lowering=False, detect_race_conditions=False)

    xT = nc.dram_tensor("xT", [D, T], F32, kind="ExternalInput").ap()
    wqkv = nc.dram_tensor("wqkv", [D, DI], F32, kind="ExternalInput").ap()
    bqkv = nc.dram_tensor("bqkv", [128, 24], F32, kind="ExternalInput").ap()
    woT = nc.dram_tensor("woT", [D, D], F32, kind="ExternalInput").ap()
    bo = nc.dram_tensor("bo", [128, 8], F32, kind="ExternalInput").ap()
    wup = nc.dram_tensor("wup", [DI, D], F32, kind="ExternalInput").ap()
    wgate = nc.dram_tensor("wgate", [DI, D], F32, kind="ExternalInput").ap()
    wdown = nc.dram_tensor("wdown", [D, DI], F32, kind="ExternalInput").ap()
    identD = nc.dram_tensor("identD", [128, 128], F32, kind="ExternalInput").ap()
    id2D = nc.dram_tensor("id2D", [128, 64], F32, kind="ExternalInput").ap()
    cmaskD = nc.dram_tensor("cmaskD", [128, 896], F32, kind="ExternalInput").ap()
    yT = nc.dram_tensor("yT", [D, T], F32, kind="ExternalOutput").ap()
    dbg_rows = nc.dram_tensor("dbg_rows", [4, T], F32, kind="ExternalOutput").ap()
    dbg_sc = nc.dram_tensor("dbg_sc", [128, 24], F32, kind="ExternalOutput").ap()
    dbg_mx = nc.dram_tensor("dbg_mx", [128, 8], F32, kind="ExternalOutput").ap()
    dbg_ln = nc.dram_tensor("dbg_ln", [128, T], F32, kind="ExternalOutput").ap()
    dbg_bb = nc.dram_tensor("dbg_bb", [128, 24], F32, kind="ExternalOutput").ap()
    dbg_id = nc.dram_tensor("dbg_id", [128, 128], F32, kind="ExternalOutput").ap()
    dbg_on = nc.dram_tensor("dbg_on", [128, 1], F32, kind="ExternalOutput").ap()

    sp_wupT = nc.dram_tensor("sp_wupT", [D, DI], F32).ap()
    sp_wgT = nc.dram_tensor("sp_wgT", [D, DI], F32).ap()
    sp_wdT = nc.dram_tensor("sp_wdT", [DI, D], F32).ap()
    sp_up = nc.dram_tensor("sp_up", [DI, T], F32).ap()
    sp_gate = nc.dram_tensor("sp_gate", [DI, T], F32).ap()
    sp_p = nc.dram_tensor("sp_p", [DI, T], F32).ap()
    sp_q4 = nc.dram_tensor("sp_q4", [DI, T], I16).ap()
    sp_u = nc.dram_tensor("sp_u", [DI, T], I16).ap()
    sp_x = nc.dram_tensor("sp_x", [D, T], F32).ap()
    sp_rows = nc.dram_tensor("sp_rows", [4, T], F32).ap()
    sp_den = nc.dram_tensor("sp_den", [16, T], F32).ap()
    arin = [nc.dram_tensor(f"arin{i}", [2, 1], F32) for i in range(8)]
    arout = [nc.dram_tensor(f"arout{i}", [2, 1], F32) for i in range(8)]

    with contextlib.ExitStack() as es:
        sb = lambda name, shape, dt=F32: es.enter_context(nc.sbuf_tensor(name, shape, dt))
        bufA = sb("bufA", [128, NCH * T])
        bufB = sb("bufB", [128, NCH * T])
        wbuf = sb("wbuf", [128, 2048])
        scr = sb("scr", [128, SCR_COLS])
        c16a = sb("c16a", [128, 1024], I16)
        c16b = sb("c16b", [128, 1024], I16)
        c16c = sb("c16c", [128, 1024], I16)
        c16d = sb("c16d", [128, 1024], I16)
        mx = sb("mx", [128, 8])
        sc = sb("sc", [128, 24])
        bbuf = sb("bbuf", [128, 24])
        obuf = sb("obuf", [128, 8])
        idb = sb("idb", [128, 128])
        onec = sb("onec", [128, 1])
        pb = [es.enter_context(nc.psum_tensor(f"pb{i}", [128, 512], F32)) for i in range(8)]
        sem_tm = Ctr(es.enter_context(nc.semaphore("tm")))
        sem_vv = Ctr(es.enter_context(nc.semaphore("vv")))
        sem_aa = Ctr(es.enter_context(nc.semaphore("aa")))
        sem_sq = Ctr(es.enter_context(nc.semaphore("sq")))
        sem_gq = Ctr(es.enter_context(nc.semaphore("gq")))
        sem_cc = Ctr(es.enter_context(nc.semaphore("cc")))
        sem_w = [Ctr(es.enter_context(nc.semaphore(f"sw{i}"))) for i in range(2)]
        sem_in = [Ctr(es.enter_context(nc.semaphore(f"si{i}"))) for i in range(2)]
        sem_sp = [Ctr(es.enter_context(nc.semaphore(f"ss{i}"))) for i in range(2)]
        sem_dq = [Ctr(es.enter_context(nc.semaphore(f"sd{i}"))) for i in range(2)]
        sem_dt = [Ctr(es.enter_context(nc.semaphore(f"st{i}"))) for i in range(2)]
        sem_ln = Ctr(es.enter_context(nc.semaphore("sln")))
        sem_xo = Ctr(es.enter_context(nc.semaphore("sxo")))
        bar = {e: Ctr(es.enter_context(nc.semaphore("b_" + e))) for e in Plan.ENGINES}
        block = es.enter_context(nc.Block())

        P = Plan()

        def dma(eng, out, in_, sem):
            val = sem.inc(16)
            P.emit(eng, lambda e, o=out, i=in_, s=sem.h: e.dma_start(out=o, in_=i).then_inc(s, 16))
            return val

        def op(eng, fn, sem=None, inc=1):
            if sem is None:
                P.emit(eng, lambda e, f=fn: f(e))
                return None
            val = sem.inc(inc)
            P.emit(eng, lambda e, f=fn, s=sem.h, n=inc: f(e).then_inc(s, n))
            return val

        def vs(fn, sem=None):
            v = op("vector", fn, sem)
            op("vector", lambda e: e.drain())
            return v

        def vrecip(out_ap, in_ap, sem=None):
            op("vector", lambda e: e.drain())
            v = op("vector", lambda e, o=out_ap, i=in_ap: e.reciprocal(out=o, in_=i), sem)
            op("vector", lambda e: e.drain())
            return v

        def barrier():
            vals = {}
            for eng in Plan.ENGINES:
                vals[eng] = bar[eng].inc(1)
                P.emit(eng, lambda e, s=bar[eng].h: e.nop().then_inc(s, 1))
            for eng in Plan.ENGINES:
                for other in Plan.ENGINES:
                    if other != eng:
                        P.wait(eng, bar[other], vals[other])

        def S(lo, hi, dt=None):
            ap = scr[:, lo:hi]
            return ap.bitcast(dt) if dt is not None else ap

        chA = [bufA[:, 2048 * c:2048 * (c + 1)] for c in range(NCH)]
        chB = [bufB[:, 2048 * c:2048 * (c + 1)] for c in range(NCH)]

        for c in range(NCH):
            dma("sync", chA[c], xT[128 * c:128 * (c + 1), :], sem_sq)
        dma("sync", bbuf[:], bqkv, sem_sq)
        dma("sync", obuf[:], bo, sem_sq)
        dma("sync", idb[:], identD, sem_sq)
        cmb = scr[:, 10784:11680]
        v_init = dma("sync", cmb, cmaskD, sem_sq)
        op("vector", lambda e: e.memset(onec[:], 1.0), sem_vv)
        for eng in ("vector", "scalar", "tensor", "gpsimd"):
            P.wait(eng, sem_sq, v_init)
        barrier()

        def dequant_matrix(src, dst, O, I_, half_cols):
            nb = half_cols // 32
            nhalf = I_ // half_cols
            it = 0
            vq_hist = {}
            for oc in range(O // 128):
                for hf in range(nhalf):
                    win = S(1536 * (it % 2), 1536 * (it % 2) + half_cols)
                    if it >= 2:
                        P.wait("sync", sem_vv, vq_hist[it - 2])
                    vin = dma("sync", win,
                              src[128 * oc:128 * (oc + 1),
                                  half_cols * hf:half_cols * (hf + 1)], sem_dq[it % 2])
                    P.wait("vector", sem_dq[it % 2], vin)
                    bm = S(3072, 3072 + nb)
                    rb = S(3168, 3168 + nb)
                    t7 = S(3264, 3264 + nb)
                    s_ = S(3360, 3360 + nb)
                    irq = S(3456, 3456 + half_cols, I32)
                    irqf = S(3456, 3456 + half_cols)
                    wp = S(4992, 4992 + half_cols)
                    win3 = win.rearrange("p (b r) -> p b r", r=32)
                    t73 = bass.AP(t7.tensor, t7.offset, [[t7.ap[0][0], 128], [1, nb], [0, 32]])
                    s_3 = bass.AP(s_.tensor, s_.offset, [[s_.ap[0][0], 128], [1, nb], [0, 32]])
                    vs(lambda e, a=bm, b=win3: e.tensor_reduce(
                        out=a, in_=b, axis=AX.X, op=ALU.max, apply_absolute_value=True))
                    vrecip(rb, bm)
                    vs(lambda e, a=t7, b=rb: e.tensor_scalar(
                        out=a, in0=b, scalar1=7.0, scalar2=None, op0=ALU.mult))
                    vs(lambda e, a=s_, b=bm: e.tensor_scalar(
                        out=a, in0=b, scalar1=INV7, scalar2=1e-12, op0=ALU.mult, op1=ALU.max))
                    op("vector", lambda e, a=irq, b=win3, c=t73: e.tensor_tensor(
                        out=a.rearrange("p (b r) -> p b r", r=32), in0=b, in1=c, op=ALU.mult))
                    op("vector", lambda e, a=irq, o=irqf: e.tensor_scalar(
                        out=o, in0=a, scalar1=7.0, scalar2=-8.0, op0=ALU.min, op1=ALU.max))
                    vq = op("vector", lambda e, a=wp, b=irqf, c=s_3: e.tensor_tensor(
                        out=a.rearrange("p (b r) -> p b r", r=32),
                        in0=b.rearrange("p (b r) -> p b r", r=32), in1=c, op=ALU.mult),
                        sem_vv)
                    vq_hist[it] = vq
                    for ict in range(half_cols // 128):
                        bank = pb[4 + (ict % 2)]
                        tev = S(6528 + 128 * (ict % 2), 6656 + 128 * (ict % 2))
                        P.wait("tensor", sem_vv, vq)
                        tv = op("tensor", lambda e, o=bank,
                                i=wp[:, 128 * ict:128 * (ict + 1)]:
                                e.transpose(o[0:128, 0:128], i, idb[:]), sem_tm)
                        P.wait("vector", sem_dt[ict % 2], sem_dt[ict % 2].v)
                        P.wait("vector", sem_tm, tv)
                        ve = op("vector", lambda e, o=tev, i=bank: e.tensor_copy(
                            out=o, in_=i[0:128, 0:128]), sem_vv)
                        P.wait("tensor", sem_vv, ve)
                        P.wait("sync", sem_vv, ve)
                        ic_glob = hf * (half_cols // 128) + ict
                        dma("sync", dst[128 * ic_glob:128 * (ic_glob + 1),
                                    128 * oc:128 * (oc + 1)], tev, sem_dt[ict % 2])
                    it += 1
        if UPTO >= 1:
            dequant_matrix(wup, sp_wupT, DI, D, 1024)
            dequant_matrix(wgate, sp_wgT, DI, D, 1024)
            dequant_matrix(wdown, sp_wdT, D, DI, 1536)
            barrier()

        def ar_round(idx, k, sc_dst, mx_ready):
            P.wait("tensor", sem_vv, mx_ready)
            tv = op("tensor", lambda e, kk=k: e.transpose(
                pb[4][0:kk, 0:128], mx[:, 0:kk], idb[:]), sem_tm)
            P.wait("vector", sem_tm, tv)
            rv = vs(lambda e, kk=k: e.tensor_reduce(
                out=mx[0:kk, 6:7], in_=pb[4][0:kk, 0:128], axis=AX.X, op=ALU.max), sem_vv)
            P.wait("gpsimd", sem_vv, rv)
            iv = dma("gpsimd", arin[idx].ap()[0:k, 0:1], mx[0:k, 6:7], sem_gq)
            P.wait("gpsimd", sem_gq, iv)
            if NCORES == 1:
                cv = dma("gpsimd", arout[idx].ap()[0:k, 0:1],
                         arin[idx].ap()[0:k, 0:1], sem_gq)
                P.wait("gpsimd", sem_gq, cv)
            else:
                cv = op("gpsimd", lambda e, i=idx, kk=k: e.collective_compute(
                    "AllReduce", ALU.max, replica_groups=[list(range(NCORES))],
                    ins=[arin[i].ap()[0:kk, 0:1].opt()],
                    outs=[arout[i].ap()[0:kk, 0:1].opt()]), sem_cc)
                P.wait("gpsimd", sem_cc, cv)
            bv = dma("gpsimd", sc[:, sc_dst:sc_dst + k],
                     arout[idx].ap()[0:k, 0:1].rearrange("a b -> (a b)")
                     .partition_broadcast(128), sem_gq)
            return bv

        def ln_phase(src, dst, quant, ar_idx):
            vv1_prev = 0
            for t in range(NT):
                tsl = slice(512 * t, 512 * (t + 1))
                mv_hist = []
                P.wait("tensor", sem_vv, vv1_prev)
                for c in range(NCH):
                    x2 = S(512 * (c % 2), 512 * (c % 2) + 512)
                    if c >= 2:
                        P.wait("scalar", sem_tm, mv_hist[c - 2])
                    av = op("scalar", lambda e, o=x2, i=src[c][:, tsl]: e.activation(
                        out=o, in_=i, func=AF.Square), sem_aa)
                    P.wait("tensor", sem_aa, av)
                    op("tensor", lambda e, i=src[c][:, tsl], st=(c == 0), sp=(c == NCH - 1):
                       e.matmul(pb[6][0:1, 0:512], onec[:], i, start=st, stop=sp))
                    mv = op("tensor", lambda e, i=x2, st=(c == 0), sp=(c == NCH - 1):
                            e.matmul(pb[7][0:1, 0:512], onec[:], i, start=st, stop=sp), sem_tm)
                    mv_hist.append(mv)
                P.wait("vector", sem_tm, mv)
                r_mu, r_m2 = S(1024, 1536), S(1536, 2048)
                r_t, r_a, r_c = S(2048, 2560), S(2560, 3072), S(3072, 3584)
                vs(lambda e, o=r_mu[0:1, :]: e.tensor_scalar(
                    out=o, in0=pb[6][0:1, 0:512], scalar1=INV1024, scalar2=None, op0=ALU.mult))
                vs(lambda e, o=r_m2[0:1, :]: e.tensor_scalar(
                    out=o, in0=pb[7][0:1, 0:512], scalar1=INV1024, scalar2=None, op0=ALU.mult))
                vs(lambda e, o=r_t[0:1, :], a=r_mu[0:1, :]: e.tensor_tensor(
                    out=o, in0=a, in1=a, op=ALU.mult))
                vv1 = vs(lambda e, o=r_m2[0:1, :], b=r_t[0:1, :]: e.tensor_tensor(
                    out=o, in0=o, in1=b, op=ALU.subtract), sem_vv)
                vv1_prev = vv1
                P.wait("scalar", sem_vv, vv1)
                av = op("scalar", lambda e, o=r_t[0:1, :], i=r_m2[0:1, :]: e.activation(
                    out=o, in_=i, func=AF.Sqrt, bias=np.float32(1e-5)), sem_aa)
                P.wait("vector", sem_aa, av)
                vrecip(r_a[0:1, :], r_t[0:1, :])
                vv2 = vs(lambda e, o=r_c[0:1, :], a=r_mu[0:1, :], b=r_a[0:1, :]:
                         e.scalar_tensor_tensor(out=o, in0=a, scalar=-1.0, in1=b,
                                                op0=ALU.mult, op1=ALU.mult), sem_vv)
                P.wait("sync", sem_vv, vv2)
                dma("sync", sp_rows[0:1, tsl], r_a[0:1, :], sem_ln)
                dma("sync", sp_rows[1:2, tsl], r_c[0:1, :], sem_ln)
            lastv = 0
            for t in range(NT):
                tsl = slice(512 * t, 512 * (t + 1))
                bca, bcc = S(4096, 4608), S(4608, 5120)
                P.wait("sync", sem_vv, lastv)
                dma("sync", bca, sp_rows[0:1, tsl].partition_broadcast(128), sem_ln)
                vb = dma("sync", bcc, sp_rows[1:2, tsl].partition_broadcast(128), sem_ln)
                P.wait("vector", sem_ln, vb)
                for c in range(NCH):
                    op("vector", lambda e, o=dst[c][:, tsl], i=src[c][:, tsl], b=bca:
                       e.tensor_tensor(out=o, in0=i, in1=b, op=ALU.mult))
                    lastv = op("vector", lambda e, o=dst[c][:, tsl], b=bcc:
                               e.tensor_tensor(out=o, in0=o, in1=b, op=ALU.add), sem_vv)
            if not quant:
                return
            if DBG:
                P.wait("sync", sem_vv, lastv)
                dma("sync", dbg_ln, dst[0], sem_sq)
            for c in range(NCH):
                dst_col = mx[:, 0:1] if c == 0 else mx[:, 7:8]
                vs(lambda e, i=dst[c], o=dst_col: e.tensor_reduce(
                    out=o, in_=i, axis=AX.X, op=ALU.max, apply_absolute_value=True))
                mv = vs(lambda e: e.tensor_tensor(
                    out=mx[:, 0:1], in0=mx[:, 0:1], in1=mx[:, 7:8], op=ALU.max), sem_vv) \
                    if c > 0 else op("vector", lambda e: e.nop(), sem_vv)
            bv = ar_round(ar_idx, 1, 0, mv)
            P.wait("vector", sem_gq, bv)
            vs(lambda e: e.tensor_scalar(
                out=sc[:, 1:2], in0=sc[:, 0:1], scalar1=INV32767, scalar2=1e-12,
                op0=ALU.mult, op1=ALU.max))
            vrecip(sc[:, 2:3], sc[:, 1:2])
            ir = S(6144, 8192, I32)
            for c in range(NCH):
                op("vector", lambda e, i=dst[c], o=ir: e.tensor_scalar(
                    out=o, in0=i, scalar1=sc[:, 2:3], scalar2=None, op0=ALU.mult))
                op("vector", lambda e, i=ir, o=dst[c]: e.tensor_copy(out=o, in_=i))

        def ffn(xreg, wreg, ar_base):
            q5flat = bufB if wreg is chB else bufA

            def stat_src(jg):
                src_w = sp_wupT if jg < NI else sp_wgT
                j = jg % NI
                return src_w[:, 128 * j:128 * (j + 1)].rearrange("(c p) j -> p c j", p=128)

            def wslot_ap(jg):
                return wbuf[:, 1024 * (jg % 2):1024 * (jg % 2) + 1024]

            pend = {0: dma("sync", wslot_ap(0).rearrange("p (c j) -> p c j", j=128),
                           stat_src(0), sem_w[0])}
            tm_after = {}
            evac_done = []
            spill_done = {}
            red_done = {}
            for jg in range(2 * NI):
                mat = jg // NI
                j = jg % NI
                spX = sp_up if mat == 0 else sp_gate
                raw = S(2048 * (jg % 2), 2048 * (jg % 2) + 2048)
                if jg + 1 < 2 * NI:
                    if jg >= 1:
                        P.wait("sync", sem_tm, tm_after[jg - 1])
                    pend[jg + 1] = dma("sync",
                                       wslot_ap(jg + 1).rearrange("p (c j) -> p c j", j=128),
                                       stat_src(jg + 1), sem_w[(jg + 1) % 2])
                P.wait("tensor", sem_w[jg % 2], pend[jg])
                for t in range(NT):
                    k = jg * NT + t
                    bank = pb[k % 2]
                    if k >= 2:
                        P.wait("tensor", sem_aa, evac_done[k - 2])
                    for c in range(NCH):
                        mmv = op("tensor", lambda e, b=bank,
                                 w=wbuf[:, 1024 * (jg % 2) + 128 * c:
                                        1024 * (jg % 2) + 128 * (c + 1)],
                                 m=wreg[c][:, 512 * t:512 * (t + 1)],
                                 st=(c == 0), sp=(c == NCH - 1):
                                 e.matmul(b[0:128, 0:512], w, m, start=st, stop=sp),
                                 sem_tm if c == NCH - 1 else None)
                    P.wait("scalar", sem_tm, mmv)
                    if jg >= 2:
                        P.wait("scalar", sem_sp[jg % 2], spill_done[jg - 2])
                        P.wait("scalar", sem_vv, red_done.get(jg - 2, 0))
                    av = op("scalar", lambda e, o=raw[:, 512 * t:512 * (t + 1)], b=bank:
                            e.copy(out=o, in_=b[0:128, 0:512]), sem_aa)
                    evac_done.append(av)
                tm_after[jg] = mmv
                P.wait("vector", sem_aa, evac_done[-1])
                first = (j == 0)
                rdst = mx[:, mat:mat + 1] if first else mx[:, 7:8]
                vs(lambda e, i=raw, o=rdst: e.tensor_reduce(
                    out=o, in_=i, axis=AX.X, op=ALU.max,
                    apply_absolute_value=True))
                red_done[jg] = (vs(lambda e, m=mat: e.tensor_tensor(
                    out=mx[:, m:m + 1], in0=mx[:, m:m + 1], in1=mx[:, 7:8], op=ALU.max),
                    sem_vv) if not first
                    else op("vector", lambda e: e.nop(), sem_vv))
                P.wait("sync", sem_aa, evac_done[-1])
                spill_done[jg] = dma("sync", spX[128 * j:128 * (j + 1), :], raw,
                                     sem_sp[jg % 2])
            fv = op("vector", lambda e: e.nop(), sem_vv)
            if FFN_UPTO < 2:
                return
            bv = ar_round(ar_base + 1, 2, 3, fv)
            P.wait("vector", sem_gq, bv)
            scv = None
            for (msrc, sdst, adst) in ((3, 5, 6), (4, 7, 8)):
                vs(lambda e, m=msrc: e.tensor_tensor(
                    out=sc[:, 18:19], in0=sc[:, 1:2], in1=sc[:, m:m + 1], op=ALU.mult))
                vs(lambda e, sd=sdst: e.tensor_scalar(
                    out=sc[:, sd:sd + 1], in0=sc[:, 18:19], scalar1=INV32767, scalar2=1e-12,
                    op0=ALU.mult, op1=ALU.max))
                vrecip(sc[:, 19:20], sc[:, sdst:sdst + 1])
                scv = vs(lambda e, ad=adst: e.tensor_tensor(
                    out=sc[:, ad:ad + 1], in0=sc[:, 1:2], in1=sc[:, 19:20], op=ALU.mult),
                    sem_vv)
            P.wait("scalar", sem_vv, scv)
            barrier()
            if FFN_UPTO < 3:
                return
            g_done, p_done, p_spill = {}, {}, {}
            for hc in range(2 * NI):
                j, hf = hc // 2, hc % 2
                par = hc % 2
                gin = S(1024 * par, 1024 * par + 1024)
                gi = S(2048 + 1024 * par, 2048 + 1024 * par + 1024, I32)
                sg = S(4096 + 1024 * par, 4096 + 1024 * par + 1024)
                ti = S(6144 + 1024 * par, 6144 + 1024 * par + 1024, I32)
                pp = S(8192 + 1024 * par, 8192 + 1024 * par + 1024)
                if hc >= 2:
                    P.wait("sync", sem_aa, g_done[hc - 2])
                gin_ok = dma("sync", gin,
                             sp_gate[128 * j:128 * (j + 1), 1024 * hf:1024 * (hf + 1)],
                             sem_in[par])
                P.wait("scalar", sem_in[par], gin_ok)
                if hc >= 2:
                    P.wait("scalar", sem_vv, p_done[hc - 2])
                g_done[hc] = op("scalar", lambda e, o=gi, i=gin: e.activation(
                    out=o, in_=i, func=AF.Copy, scale=sc[:, 8:9], bias=0.0), sem_aa)
                op("scalar", lambda e, o=sg, i=gi: e.activation(
                    out=o, in_=i, func=AF.Sigmoid, scale=sc[:, 7:8]))
                tdone = op("scalar", lambda e, o=ti, i=sg: e.activation(
                    out=o, in_=i, func=AF.Copy, scale=np.float32(65536.0), bias=0.0), sem_aa)
                P.wait("vector", sem_aa, tdone)
                if hc >= 2:
                    P.wait("vector", sem_sp[par], p_spill[hc - 2])
                op("vector", lambda e, o=pp, a=gi, b=ti: e.tensor_tensor(
                    out=o, in0=a, in1=b, op=ALU.mult))
                rdst = mx[:, 0:1] if hc == 0 else mx[:, 7:8]
                vs(lambda e, i=pp, o=rdst: e.tensor_reduce(
                    out=o, in_=i, axis=AX.X, op=ALU.max, apply_absolute_value=True))
                p_done[hc] = (vs(lambda e: e.tensor_tensor(
                    out=mx[:, 0:1], in0=mx[:, 0:1], in1=mx[:, 7:8], op=ALU.max), sem_vv)
                    if hc > 0 else op("vector", lambda e: e.nop(), sem_vv))
                P.wait("sync", sem_vv, p_done[hc])
                p_spill[hc] = dma("sync",
                                  sp_p[128 * j:128 * (j + 1), 1024 * hf:1024 * (hf + 1)],
                                  pp, sem_sp[par])
            if FFN_UPTO < 4:
                return
            bv = ar_round(ar_base + 2, 1, 9, p_done[2 * NI - 1])
            P.wait("vector", sem_gq, bv)
            vs(lambda e: e.tensor_scalar(
                out=sc[:, 10:11], in0=sc[:, 7:8], scalar1=INV65536, scalar2=None, op0=ALU.mult))
            vs(lambda e: e.tensor_tensor(
                out=sc[:, 18:19], in0=sc[:, 10:11], in1=sc[:, 9:10], op=ALU.mult))
            vs(lambda e: e.tensor_scalar(
                out=sc[:, 11:12], in0=sc[:, 18:19], scalar1=INV32767, scalar2=1e-12,
                op0=ALU.mult, op1=ALU.max))
            vrecip(sc[:, 19:20], sc[:, 11:12])
            vs(lambda e: e.tensor_tensor(
                out=sc[:, 12:13], in0=sc[:, 10:11], in1=sc[:, 19:20], op=ALU.mult))
            barrier()
            if FFN_UPTO < 5:
                return
            c_done, c_spill = {}, {}
            for hc in range(2 * NI):
                j, hf = hc // 2, hc % 2
                par = hc % 2
                pin = S(1024 * par, 1024 * par + 1024)
                uin = S(2048 + 1024 * par, 2048 + 1024 * par + 1024)
                q4s = (c16a if par == 0 else c16b)[:]
                us = (c16c if par == 0 else c16d)[:]
                p2 = S(4096, 5120)
                if hc >= 2:
                    P.wait("sync", sem_vv, c_done[hc - 2])
                dma("sync", pin, sp_p[128 * j:128 * (j + 1), 1024 * hf:1024 * (hf + 1)],
                    sem_in[par])
                pin_ok = dma("sync", uin,
                             sp_up[128 * j:128 * (j + 1), 1024 * hf:1024 * (hf + 1)],
                             sem_in[par])
                P.wait("vector", sem_in[par], pin_ok)
                if hc >= 2:
                    P.wait("vector", sem_sp[par], c_spill[hc - 2])
                op("vector", lambda e, o=q4s, i=pin: e.tensor_scalar(
                    out=o, in0=i, scalar1=sc[:, 12:13], scalar2=None, op0=ALU.mult))
                op("vector", lambda e, o=us, i=uin: e.tensor_scalar(
                    out=o, in0=i, scalar1=sc[:, 6:7], scalar2=None, op0=ALU.mult))
                op("vector", lambda e, o=p2, a=q4s, b=us: e.tensor_tensor(
                    out=o, in0=a, in1=b, op=ALU.mult))
                rdst = mx[:, 0:1] if hc == 0 else mx[:, 7:8]
                vs(lambda e, i=p2, o=rdst: e.tensor_reduce(
                    out=o, in_=i, axis=AX.X, op=ALU.max, apply_absolute_value=True))
                c_done[hc] = (vs(lambda e: e.tensor_tensor(
                    out=mx[:, 0:1], in0=mx[:, 0:1], in1=mx[:, 7:8], op=ALU.max), sem_vv)
                    if hc > 0 else op("vector", lambda e: e.nop(), sem_vv))
                P.wait("sync", sem_vv, c_done[hc])
                dma("sync", sp_q4[128 * j:128 * (j + 1), 1024 * hf:1024 * (hf + 1)], q4s,
                    sem_sp[par])
                c_spill[hc] = dma("sync",
                                  sp_u[128 * j:128 * (j + 1), 1024 * hf:1024 * (hf + 1)],
                                  us, sem_sp[par])
            if FFN_UPTO < 6:
                return
            bv = ar_round(ar_base + 3, 1, 13, c_done[2 * NI - 1])
            P.wait("vector", sem_gq, bv)
            vs(lambda e: e.tensor_tensor(
                out=sc[:, 14:15], in0=sc[:, 11:12], in1=sc[:, 5:6], op=ALU.mult))
            vs(lambda e: e.tensor_tensor(
                out=sc[:, 18:19], in0=sc[:, 14:15], in1=sc[:, 13:14], op=ALU.mult))
            vs(lambda e: e.tensor_scalar(
                out=sc[:, 15:16], in0=sc[:, 18:19], scalar1=INV32767, scalar2=1e-12,
                op0=ALU.mult, op1=ALU.max))
            vrecip(sc[:, 19:20], sc[:, 15:16])
            vs(lambda e: e.tensor_tensor(
                out=sc[:, 16:17], in0=sc[:, 14:15], in1=sc[:, 19:20], op=ALU.mult))
            vs(lambda e: e.tensor_scalar(
                out=sc[:, 17:18], in0=sc[:, 15:16], scalar1=0.5, scalar2=None, op0=ALU.mult))
            barrier()
            if FFN_UPTO < 7:
                return
            for t in range(NT):
                tsl = slice(512 * t, 512 * (t + 1))
                q5_ready = []
                for j in range(NI):
                    par = j % 2
                    q4in = (c16a if par == 0 else c16b)[:][:, 0:512]
                    uin = (c16c if par == 0 else c16d)[:][:, 0:512]
                    p2d = S(1024, 1536)
                    i32d = S(1536, 2048, I32)
                    if j >= 2:
                        P.wait("sync", sem_vv, q5_ready[j - 2])
                    dma("sync", q4in, sp_q4[128 * j:128 * (j + 1), tsl], sem_in[par])
                    uok = dma("sync", uin, sp_u[128 * j:128 * (j + 1), tsl], sem_in[par])
                    P.wait("vector", sem_in[par], uok)
                    op("vector", lambda e, o=p2d, a=q4in, b=uin: e.tensor_tensor(
                        out=o, in0=a, in1=b, op=ALU.mult))
                    op("vector", lambda e, o=i32d, i=p2d: e.tensor_scalar(
                        out=o, in0=i, scalar1=sc[:, 16:17], scalar2=None, op0=ALU.mult))
                    qv = op("vector", lambda e, o=q5flat[:, 512 * j:512 * (j + 1)], i=i32d:
                            e.tensor_copy(out=o, in_=i), sem_vv)
                    q5_ready.append(qv)
                wd_pend = {}
                wd_tm = {}
                for cpre in range(2):
                    wd_pend[cpre] = dma("sync",
                                        wbuf[:, 1024 * (cpre % 2):1024 * (cpre % 2) + 1024],
                                        sp_wdT[128 * cpre:128 * (cpre + 1), :],
                                        sem_w[cpre % 2])
                for c in range(NI):
                    P.wait("tensor", sem_w[c % 2], wd_pend[c])
                    P.wait("tensor", sem_vv, q5_ready[c])
                    for h in range(NCH):
                        mmv = op("tensor", lambda e, b=pb[h],
                                 w=wbuf[:, 1024 * (c % 2) + 128 * h:
                                        1024 * (c % 2) + 128 * (h + 1)],
                                 m=q5flat[:, 512 * c:512 * (c + 1)],
                                 st=(c == 0), sp=(c == NI - 1):
                                 e.matmul(b[0:128, 0:512], w, m, start=st, stop=sp),
                                 sem_tm if h == NCH - 1 else None)
                    wd_tm[c] = mmv
                    if c + 2 < NI:
                        P.wait("sync", sem_tm, wd_tm[c])
                        wd_pend[c + 2] = dma("sync",
                                             wbuf[:, 1024 * (c % 2):1024 * (c % 2) + 1024],
                                             sp_wdT[128 * (c + 2):128 * (c + 3), :],
                                             sem_w[c % 2])
                P.wait("vector", sem_tm, wd_tm[NI - 1])
                ev = None
                for h in range(NCH):
                    ev = op("vector", lambda e, b=pb[h], x=xreg[h][:, tsl]:
                            e.scalar_tensor_tensor(out=x, in0=b[0:128, 0:512],
                                                   scalar=sc[:, 17:18], in1=x,
                                                   op0=ALU.mult, op1=ALU.add), sem_vv)
                P.wait("tensor", sem_vv, ev)
                barrier()

        def attention():
            QB, KB = S(0, 2048), S(2048, 4096)
            VTMP = S(4096, 6144)
            ET = [S(8224, 8736), S(8736, 9248)]
            STG, STG2 = S(9248, 9760), S(9760, 10272)
            BCD = S(10272, 10784)
            xs = None
            for c in range(NCH):
                xs = dma("sync", sp_x[128 * c:128 * (c + 1), :], chA[c], sem_sq)
            ov = None
            for kt in range(16):
                op("vector", lambda e, k=kt: e.memset(
                    S(6144 + 130 * k + 64, 6144 + 130 * k + 65), 1.0))
                ov = op("vector", lambda e, k=kt: e.memset(
                    S(6144 + 130 * k + 129, 6144 + 130 * k + 130), 1.0), sem_vv)
            P.wait("tensor", sem_vv, ov)
            P.wait("vector", sem_sq, xs)

            strip_tm = {}
            strip_pend = {0: dma("sync",
                                 wbuf[:, 0:1024].rearrange("p (c j) -> p c j", j=128),
                                 wqkv[:, 0:128].rearrange("(c p) j -> p c j", p=128),
                                 sem_w[0])}
            for p in range(NCH):
                if ATT_UPTO < 2:
                    break
                ctp = bufA[:, 2048 * p:2048 * p + 2048]
                for js in range(3):
                    Sg = 3 * p + js
                    wslot = wbuf[:, 1024 * (Sg % 2):1024 * (Sg % 2) + 1024]
                    if Sg + 1 < 24:
                        if Sg >= 1:
                            P.wait("sync", sem_tm, strip_tm[Sg - 1])
                        strip_pend[Sg + 1] = dma(
                            "sync",
                            wbuf[:, 1024 * ((Sg + 1) % 2):1024 * ((Sg + 1) % 2) + 1024]
                            .rearrange("p (c j) -> p c j", j=128),
                            wqkv[:, 128 * (Sg + 1):128 * (Sg + 2)]
                            .rearrange("(c p) j -> p c j", p=128), sem_w[(Sg + 1) % 2])
                    P.wait("tensor", sem_w[Sg % 2], strip_pend[Sg])
                    dest = (QB, KB, VTMP)[js]
                    ev_hist = []
                    for t in range(NT):
                        k = t
                        bank = pb[k % 2]
                        if k >= 2:
                            P.wait("tensor", sem_vv, ev_hist[k - 2])
                        for c in range(NCH):
                            mmv = op("tensor", lambda e, b=bank,
                                     w=wslot[:, 128 * c:128 * (c + 1)],
                                     m=chB[c][:, 512 * t:512 * (t + 1)],
                                     st=(c == 0), sp=(c == NCH - 1):
                                     e.matmul(b[0:128, 0:512], w, m, start=st, stop=sp),
                                     sem_tm if c == NCH - 1 else None)
                        P.wait("vector", sem_tm, mmv)
                        evv = op("vector", lambda e, o=dest[:, 512 * t:512 * (t + 1)], b=bank,
                                 bc=bbuf[:, Sg:Sg + 1]:
                                 e.tensor_scalar(out=o, in0=b[0:128, 0:512], scalar1=bc,
                                                 scalar2=None, op0=ALU.add), sem_vv)
                        ev_hist.append(evv)
                    strip_tm[Sg] = mmv
                    P.wait("tensor", sem_vv, ev_hist[-1])
                    P.wait("tensor", sem_vv, ev_hist[-2])
                for kt in range(16 if ATT_UPTO >= 3 else 0):
                    tvb = op("tensor", lambda e, k=kt: e.transpose(
                        pb[5][0:128, 0:128], VTMP[:, 128 * k:128 * (k + 1)], idb[:]),
                        sem_tm)
                    P.wait("vector", sem_tm, tvb)
                    op("vector", lambda e, k=kt: e.tensor_copy(
                        out=S(6144 + 130 * k, 6144 + 130 * k + 64), in_=pb[5][0:128, 0:64]))
                    vev = op("vector", lambda e, k=kt: e.tensor_copy(
                        out=S(6144 + 130 * k + 65, 6144 + 130 * k + 129),
                        in_=pb[5][0:128, 64:128]), sem_vv)
                    P.wait("tensor", sem_vv, vev)
                prev_evac = 0
                for s in range(NT if ATT_UPTO >= 4 else 0):
                    ktmax = 4 * s + 4
                    exp_done = {}
                    pv_done = {}
                    P.wait("tensor", sem_vv, prev_evac)
                    for kt in range(ktmax):
                        for h in range(2):
                            idx = 2 * kt + h
                            base = 64 * h
                            sbank = pb[idx % 4]
                            if idx >= 4:
                                P.wait("tensor", *exp_done[idx - 4])
                            smv = op("tensor", lambda e, b=sbank, kk=kt, bb=base, ss=s:
                                     e.matmul(b[0:128, 0:512],
                                              KB[bb:bb + 64, 128 * kk:128 * (kk + 1)],
                                              QB[bb:bb + 64, 512 * ss:512 * (ss + 1)],
                                              start=True, stop=True), sem_tm)
                            P.wait("scalar", sem_tm, smv)
                            if idx >= 2:
                                P.wait("scalar", sem_tm, pv_done[idx - 2])
                            eslot = ET[idx % 2]
                            avv = op("scalar", lambda e, o=eslot, b=sbank: e.activation(
                                out=o, in_=b[0:128, 0:512], func=AF.Exp,
                                scale=np.float32(0.125)), sem_aa)
                            if kt >= 4 * s:
                                d = 128 * (kt - 4 * s)
                                P.wait("vector", sem_aa, avv)
                                mvv = op("vector", lambda e, o=eslot, dd=d: e.tensor_tensor(
                                    out=o, in0=o, in1=cmb[:, 384 - dd:896 - dd],
                                    op=ALU.mult), sem_vv)
                                exp_done[idx] = (sem_vv, mvv)
                            else:
                                exp_done[idx] = (sem_aa, avv)
                            P.wait("tensor", *exp_done[idx])
                            if ATT_UPTO < 5:
                                pv_done[idx] = None
                                continue
                            vt = S(6144 + 130 * kt + 65 * h, 6144 + 130 * kt + 65 * h + 65)
                            pv_done[idx] = op("tensor", lambda e, b=pb[6 + h], w=vt, m=eslot,
                                              st=(kt == 0), sp=(kt == ktmax - 1):
                                              e.matmul(b[0:65, 0:512], w, m, start=st, stop=sp),
                                              sem_tm)
                    if ATT_UPTO < 5:
                        continue
                    tsl = slice(512 * s, 512 * (s + 1))
                    P.wait("vector", sem_tm, pv_done[2 * (ktmax - 1) + 1])
                    op("vector", lambda e, o=ctp[0:64, tsl]: e.tensor_copy(
                        out=o, in_=pb[6][0:64, 0:512]))
                    vrecip(STG2[64:65, 0:512], pb[6][64:65, 0:512])
                    op("vector", lambda e: e.tensor_copy(
                        out=STG[0:64, 0:512], in_=pb[7][0:64, 0:512]))
                    evv = vrecip(STG[64:65, 0:512], pb[7][64:65, 0:512], sem_vv)
                    prev_evac = evv
                    P.wait("gpsimd", sem_vv, evv)
                    dma("gpsimd", ctp[64:128, tsl], STG[0:64, 0:512], sem_gq)
                    dma("gpsimd", sp_den[2 * p:2 * p + 1, tsl], STG2[64:65, 0:512], sem_gq)
                    gv = dma("gpsimd", sp_den[2 * p + 1:2 * p + 2, tsl],
                             STG[64:65, 0:512], sem_gq)
                    P.wait("vector", sem_gq, gv)
                nv = 0
                for s in range(NT if ATT_UPTO >= 6 else 0):
                    tsl = slice(512 * s, 512 * (s + 1))
                    P.wait("gpsimd", sem_vv, nv)
                    dma("gpsimd", BCD[0:64, 0:512],
                        sp_den[2 * p:2 * p + 1, tsl].partition_broadcast(64), sem_gq)
                    gv = dma("gpsimd", BCD[64:128, 0:512],
                             sp_den[2 * p + 1:2 * p + 2, tsl].partition_broadcast(64), sem_gq)
                    P.wait("vector", sem_gq, gv)
                    nv = op("vector", lambda e, o=ctp[:, tsl]: e.tensor_tensor(
                        out=o, in0=o, in1=BCD[:, 0:512], op=ALU.mult), sem_vv)
            barrier()
            XOLD = [S(0, 512), S(512, 1024), S(1024, 1536), S(1536, 2048)]
            for t in range(NT if ATT_UPTO >= 7 else 0):
                tsl = slice(512 * t, 512 * (t + 1))
                wo_pend = {}
                wo_tm = {}
                for cpre in range(2):
                    wo_pend[cpre] = dma("sync",
                                        wbuf[:, 1024 * (cpre % 2):1024 * (cpre % 2) + 1024],
                                        woT[128 * cpre:128 * (cpre + 1), :], sem_w[cpre % 2])
                for c in range(NCH):
                    P.wait("tensor", sem_w[c % 2], wo_pend[c])
                    for jo in range(NCH):
                        mmv = op("tensor", lambda e, b=pb[jo],
                                 w=wbuf[:, 1024 * (c % 2) + 128 * jo:
                                        1024 * (c % 2) + 128 * (jo + 1)],
                                 m=bufA[:, 2048 * c:2048 * c + 2048][:, tsl],
                                 st=(c == 0), sp=(c == NCH - 1):
                                 e.matmul(b[0:128, 0:512], w, m, start=st, stop=sp),
                                 sem_tm if jo == NCH - 1 else None)
                    wo_tm[c] = mmv
                    if c + 2 < NCH:
                        P.wait("sync", sem_tm, wo_tm[c])
                        wo_pend[c + 2] = dma("sync",
                                             wbuf[:, 1024 * (c % 2):1024 * (c % 2) + 1024],
                                             woT[128 * (c + 2):128 * (c + 3), :],
                                             sem_w[c % 2])
                xold_ok = {}
                for jo in range(4):
                    xold_ok[jo] = dma("sync", XOLD[jo],
                                      sp_x[128 * jo:128 * (jo + 1), tsl], sem_xo)
                P.wait("vector", sem_tm, wo_tm[NCH - 1])
                ev_vals = {}
                for jo in range(NCH):
                    P.wait("vector", sem_xo, sem_xo.v)
                    ev_vals[jo] = op("vector", lambda e, b=pb[jo], x=XOLD[jo % 4],
                                     o=chB[jo][:, tsl], bc=obuf[:, jo:jo + 1]:
                                     e.scalar_tensor_tensor(out=o, in0=b[0:128, 0:512],
                                                            scalar=bc, in1=x,
                                                            op0=ALU.add, op1=ALU.add), sem_vv)
                    if jo + 4 < NCH:
                        P.wait("sync", sem_vv, ev_vals[jo])
                        xold_ok[jo + 4] = dma("sync", XOLD[jo % 4],
                                              sp_x[128 * (jo + 4):128 * (jo + 5), tsl],
                                              sem_xo)
                P.wait("tensor", sem_vv, ev_vals[NCH - 1])
                barrier()

        if UPTO >= 2:
            ln_phase(chA, chB, quant=True, ar_idx=0)
            barrier()
        if UPTO >= 3:
            ffn(chA, chB, ar_base=0)
            barrier()
        if UPTO >= 4:
            ln_phase(chA, chB, quant=False, ar_idx=None)
            barrier()
        if UPTO >= 5:
            attention()
            barrier()
        if UPTO >= 6:
            ln_phase(chB, chA, quant=True, ar_idx=4)
            barrier()
        if UPTO >= 7:
            ffn(chB, chA, ar_base=4)
            barrier()
        if DBG:
            dma("sync", dbg_rows, sp_rows, sem_sq)
            dma("sync", dbg_sc, sc[:], sem_sq)
            dma("sync", dbg_mx, mx[:], sem_sq)
            dma("sync", dbg_bb, bbuf[:], sem_sq)
            dma("sync", dbg_id, idb[:], sem_sq)
            dma("sync", dbg_on, onec[:], sem_sq)
        outsrc = chB if (UPTO >= 5 or UPTO == 2 or UPTO == 4) else chA
        outv = 0
        for c in range(NCH):
            outv = dma("sync", yT[128 * c:128 * (c + 1), :], outsrc[c], sem_sq)
        P.wait("sync", sem_sq, outv)

        @block.tensor
        def _(e):
            for f in P.ops["tensor"]:
                f(e)

        @block.vector
        def _(e):
            for f in P.ops["vector"]:
                f(e)

        @block.scalar
        def _(e):
            for f in P.ops["scalar"]:
                f(e)

        @block.sync
        def _(e):
            for f in P.ops["sync"]:
                f(e)

        @block.gpsimd
        def _(e):
            for f in P.ops["gpsimd"]:
                f(e)

    return nc


_NC_CACHE = {}


def _get_nc():
    if "nc" not in _NC_CACHE:
        _NC_CACHE["nc"] = build_program()
    return _NC_CACHE["nc"]


def kernel(L0_attn_encodings, L0_attn_attn_w, L0_attn_attn_b, L0_attn_out_w,
           L0_attn_out_b, L0_ffn_w_gate, L0_ffn_w_up, L0_ffn_w_down):
    x = np.ascontiguousarray(np.asarray(L0_attn_encodings, dtype=np.float32))
    attn_w = np.ascontiguousarray(np.asarray(L0_attn_attn_w, dtype=np.float32))
    attn_b = np.ascontiguousarray(np.asarray(L0_attn_attn_b, dtype=np.float32))
    out_w = np.ascontiguousarray(np.asarray(L0_attn_out_w, dtype=np.float32))
    out_b = np.ascontiguousarray(np.asarray(L0_attn_out_b, dtype=np.float32))
    w_gate = np.ascontiguousarray(np.asarray(L0_ffn_w_gate, dtype=np.float32))
    w_up = np.ascontiguousarray(np.asarray(L0_ffn_w_up, dtype=np.float32))
    w_down = np.ascontiguousarray(np.asarray(L0_ffn_w_down, dtype=np.float32))

    perm = []
    for p in range(8):
        for part in range(3):
            for hh in range(2):
                h = 2 * p + hh
                base = 192 * h + 64 * part
                perm.extend(range(base, base + 64))
    perm = np.asarray(perm)
    wqkv_perm = np.ascontiguousarray(attn_w[perm, :].T)
    bqkv_perm = np.ascontiguousarray(attn_b[perm].reshape(24, 128).T)
    woT = np.ascontiguousarray(out_w.T)
    bo_r = np.ascontiguousarray(out_b.reshape(8, 128).T)
    identD = np.eye(128, dtype=np.float32)
    id2D = np.ascontiguousarray(np.vstack([np.eye(64), np.eye(64)]).astype(np.float32))
    kk = np.arange(128)[:, None]
    cc = np.arange(896)[None, :]
    cmaskD = (kk <= cc - 384).astype(np.float32)

    nc = _get_nc()
    common = {
        "wqkv": wqkv_perm, "bqkv": bqkv_perm, "woT": woT, "bo": bo_r,
        "wup": w_up, "wgate": w_gate, "wdown": w_down,
        "identD": identD, "id2D": id2D, "cmaskD": cmaskD,
    }
    in_maps = []
    for c in range(NCORES):
        m = dict(common)
        m["xT"] = np.ascontiguousarray(x[c].T)
        in_maps.append(m)
    res = run_bass_kernel_spmd(nc, in_maps, core_ids=list(range(NCORES)))
    _NC_CACHE["res"] = res
    out = np.stack([np.ascontiguousarray(res.results[c]["yT"].T) for c in range(NCORES)])
    return out
